# version 37
# baseline (speedup 1.0000x reference)
import numpy as np
import ml_dtypes
import concourse.bass as bass
import concourse.mybir as mybir
from concourse.bass_utils import run_bass_kernel_spmd

F32 = mybir.dt.float32
BF16 = mybir.dt.bfloat16
F32R = mybir.dt.float32r
AF = mybir.ActivationFunctionType
OP = mybir.AluOpType

D = 1024
NCH = 8
H = 16
NKV = 4
N_CORES = 8
A_H = [0, 1, 2, 3, 8, 9, 10, 11]
B_H = [4, 5, 6, 7, 12, 13, 14, 15]
SIGMA = np.concatenate(
    [np.r_[64 * A_H[c]:64 * A_H[c] + 64, 64 * B_H[c]:64 * B_H[c] + 64]
     for c in range(NCH)])
CLAMP = 3.141592
_TP = 2.0 * np.pi
C1 = float(np.float32(_TP))
C2 = float(np.float32(_TP - C1))
C3 = float(np.float32(_TP - C1 - C2))


def f32_layout(T):
    o = {}
    o["X"] = 0
    o["cos"] = NCH * T
    o["sin"] = NCH * T + T // 2
    o["scr1"] = NCH * T + T
    o["scr2"] = o["scr1"] + 512
    o["ev"] = o["scr2"] + 512
    base = o["ev"] + 2048
    o["consts"] = base
    o["bq"] = base + 8
    o["bo"] = base + 16
    o["bfn"] = base + 24
    o["bk"] = base + 32
    o["jmod"] = base + 34
    o["one"] = base + 35
    o["ten4"] = base + 36
    o["onesl"] = base + 40
    o["end"] = base + 168
    return o


def bf16_layout(T):
    o = {}
    o["act"] = 0
    o["q"] = NCH * T
    o["k"] = 2 * NCH * T
    o["v"] = 2 * NCH * T + 2 * T
    o["wA"] = o["v"] + (T // 128) * 260
    o["wKv"] = o["wA"] + 8192
    o["esl"] = o["wKv"]
    o["wR1"] = o["wKv"] + 4096
    o["p128t"] = o["wR1"] + 8192
    o["qpre"] = o["p128t"] + 128
    o["onescol"] = o["qpre"] + 512
    o["rsc"] = o["onescol"] + 8
    o["ones16r"] = o["rsc"] + 512
    o["end"] = o["ones16r"] + 128
    if 2 * T + (T // 128) * 260 >= 8192:
        o["w3"] = o["k"]
    else:
        o["w3"] = o["end"]
        o["end"] += 8192
    return o


def build(T, h_limit=H):
    NFB = T // 512
    NJ = T // 128
    NQB = T // 512
    NG2 = NJ // 2
    LO = f32_layout(T)
    BO = bf16_layout(T)

    nc = bass.Bass("TRN2", target_bir_lowering=False)
    d_x = nc.dram_tensor("xT", [D, T], F32, kind="ExternalInput").ap()
    d_wq = nc.dram_tensor("wqT", [D, D], F32, kind="ExternalInput").ap()
    d_wk = nc.dram_tensor("wkT", [D, 256], F32, kind="ExternalInput").ap()
    d_wv = nc.dram_tensor("wvT", [D, 256], F32, kind="ExternalInput").ap()
    d_wo = nc.dram_tensor("woT", [D, D], F32, kind="ExternalInput").ap()
    d_w1 = nc.dram_tensor("w1", [D, D], F32, kind="ExternalInput").ap()
    d_w2 = nc.dram_tensor("w2", [D, D], F32, kind="ExternalInput").ap()
    d_w3 = nc.dram_tensor("w3", [D, D], F32, kind="ExternalInput").ap()
    d_bq = nc.dram_tensor("bq_t", [128, 8], F32, kind="ExternalInput").ap()
    d_bk = nc.dram_tensor("bk_t", [128, 2], F32, kind="ExternalInput").ap()
    d_bo = nc.dram_tensor("bo_t", [128, 8], F32, kind="ExternalInput").ap()
    d_bfn = nc.dram_tensor("bfn_t", [128, 8], F32, kind="ExternalInput").ap()
    d_bv = nc.dram_tensor("bv_row", [1, 256], F32, kind="ExternalInput").ap()
    d_jmod = nc.dram_tensor("jmod", [128, 1], F32, kind="ExternalInput").ap()
    d_p128t = nc.dram_tensor("p128t", [128, 128], BF16, kind="ExternalInput").ap()
    d_y = nc.dram_tensor("yT", [D, T], F32, kind="ExternalOutput").ap()

    SEMS = ["qX", "qC", "qW1", "qW2", "qW3", "qW4", "qW5", "qO",
            "sT", "sV", "sA", "sP"]
    DMA_SEMS = {"qX", "qC", "qW1", "qW2", "qW3", "qW4", "qW5", "qO"}
    cnt = {s: 0 for s in SEMS}
    prog = {e: [] for e in "TVAPS"}
    waited = {e: {s: 0 for s in SEMS} for e in "TVAPS"}

    def emit(eng, fn, w=(), inc=None, drain=False):
        ws = []
        for (s, v) in w:
            if v > waited[eng][s]:
                ws.append((s, v))
                waited[eng][s] = v
        if inc is not None:
            cnt[inc] += 16 if inc in DMA_SEMS else 1
        prog[eng].append((drain, tuple(ws), fn, inc))
        return cnt[inc] if inc is not None else None

    with (
        nc.sbuf_tensor("ARF", [128, LO["end"]], F32) as ARF,
        nc.sbuf_tensor("ARB", [128, BO["end"]], BF16) as ARB,
        nc.sbuf_tensor("TI", [128, 64], mybir.dt.int32) as TI,
        nc.psum_tensor("PSA", [128, 2048], F32) as PSA,
        nc.psum_tensor("PSB", [128, 2048], F32) as PSB,
    ):
        def Xc(c, fb=None):
            o = LO["X"] + c * T
            if fb is None:
                return ARF[:, o:o + T]
            return ARF[:, o + fb * 512:o + fb * 512 + 512]
        cos16 = ARF[:, LO["cos"]:LO["cos"] + T // 2].bitcast(BF16)[:, 0:T]
        sin16 = ARF[:, LO["sin"]:LO["sin"] + T // 2].bitcast(BF16)[:, 0:T]
        EVb = [ARF[:, LO["ev"] + p * 1024:LO["ev"] + p * 1024 + 1024]
               for p in (0, 1)]
        scr1 = ARF[:, LO["scr1"]:LO["scr1"] + 512]
        scr2 = ARF[:, LO["scr2"]:LO["scr2"] + 512]
        def r1(p, n=512):
            return ARF[p:p + 1, LO["scr1"]:LO["scr1"] + n]
        def r2(p, n=512):
            return ARF[p:p + 1, LO["scr2"]:LO["scr2"] + n]
        rowA = ARF[0:1, LO["scr1"]:LO["scr1"] + 512]
        rowB = ARF[0:1, LO["scr2"]:LO["scr2"] + 512]
        rowE = ARF[32:33, LO["scr2"]:LO["scr2"] + 512]
        def cvec(i):
            return ARF[:, LO["consts"] + i:LO["consts"] + i + 1]
        pib, halfpib, twopib, Lb, invf, af_v, inv2pib = (cvec(i) for i in range(7))
        bq_t = ARF[:, LO["bq"]:LO["bq"] + 8]
        bo_t = ARF[:, LO["bo"]:LO["bo"] + 8]
        bfn_t = ARF[:, LO["bfn"]:LO["bfn"] + 8]
        bk_t = ARF[:, LO["bk"]:LO["bk"] + 2]
        jmod = ARF[:, LO["jmod"]:LO["jmod"] + 1]
        one_t = ARF[0:1, LO["one"]:LO["one"] + 1]
        ten4_t = ARF[0:1, LO["ten4"]:LO["ten4"] + 1]
        def onesF(p):
            return ARF[p:p + 1, LO["onesl"]:LO["onesl"] + 128]
        onesl = onesF(0)

        ARBF = ARB[:].bitcast(F32)
        def actv(c, fb=None):
            o = BO["act"] + c * T
            if fb is None:
                return ARB[:, o:o + T]
            return ARB[:, o + fb * 512:o + fb * 512 + 512]
        def eslot(sl, jj=None):
            o = BO["esl"] + sl * 1024
            if jj is None:
                return ARB[:, o:o + 1024]
            return ARB[:, o + jj * 512:o + jj * 512 + 512]
        def qv(c, fb=None, base=0, np_=128):
            o = BO["q"] + c * T
            if fb is None:
                return ARB[base:base + np_, o:o + T]
            return ARB[base:base + np_, o + fb * 512:o + fb * 512 + 512]
        def kv_(ck, base, j):
            o = BO["k"] + ck * T + j * 128
            return ARB[base:base + 64, o:o + 128]
        def vones(j, g):
            o = BO["v"] + j * 260 + 65 * g
            return ARB[:, o:o + 128]
        def wA(kc, co):
            o = BO["wA"] + kc * D + co * 128
            return ARB[:, o:o + 128]
        def wKc(kc, co):
            o = BO["wKv"] + kc * 256 + co * 128
            return ARB[:, o:o + 128]
        def wVc(kc):
            o = BO["wKv"] + 2048 + kc * 256
            return ARB[:, o:o + 256]
        def wR1(kc, co):
            o = BO["wR1"] + kc * D + co * 128
            return ARB[:, o:o + 128]
        def wKV3(kc, co):
            o = BO["w3"] + kc * D + co * 128
            return ARB[:, o:o + 128]
        p128t = ARB[:, BO["p128t"]:BO["p128t"] + 128]
        qpre = ARB[:, BO["qpre"]:BO["qpre"] + 512]
        onescol = ARB[:, BO["onescol"]:BO["onescol"] + 1]
        rsc1 = ARB[:, BO["rsc"]:BO["rsc"] + 512]
        ones16r = ARB[0:1, BO["ones16r"]:BO["ones16r"] + 128]
        rowAB16 = ARF[:, LO["scr1"]:LO["scr1"] + 512].bitcast(BF16)[0:1, 0:1024]
        qf32 = BO["q"] // 2
        _rm = (BO["q"] + NCH * T) // 2 - 1024
        rowM = ARBF[0:1, _rm:_rm + 512]
        bvrow = ARBF[0:1, qf32 + 3 * T:qf32 + 3 * T + 256]
        T0 = ARBF[:, qf32:qf32 + T]
        T1 = ARBF[:, qf32 + T:qf32 + 2 * T]
        T2 = ARBF[:, qf32 + 2 * T:qf32 + 3 * T]
        def psA(b, n=512):
            return PSA[:, b * 512:b * 512 + n]
        def psB(b, n=512):
            return PSB[:, b * 512:b * 512 + n]
        FACC = PSB[:, 1024:1536]
        FROT = PSB[:, 1536:2048]

        for dst, src in ((bq_t, d_bq), (bk_t, d_bk), (bo_t, d_bo),
                         (bfn_t, d_bfn), (bvrow, d_bv), (jmod, d_jmod),
                         (p128t, d_p128t)):
            emit("S", lambda e, a=dst, b=src: e.dma_start(out=a, in_=b), inc="qC")
        qC_done = cnt["qC"]
        for c in range(NCH):
            emit("S", lambda e, c_=c: e.dma_start(
                out=Xc(c_), in_=d_x[c_ * 128:c_ * 128 + 128, :]), inc="qX")
        emit("P", lambda e: e.dma_start(
            out=ARB[:, BO["wA"]:BO["wA"] + 8192].rearrange("p (kc o) -> p kc o", kc=8),
            in_=d_wq.rearrange("(kc p) o -> p kc o", p=128)), inc="qW1")
        emit("P", lambda e: e.dma_start(
            out=ARB[:, BO["wKv"]:BO["wKv"] + 2048].rearrange("p (kc o) -> p kc o", kc=8),
            in_=d_wk.rearrange("(kc p) o -> p kc o", p=128)), inc="qW1")
        emit("P", lambda e: e.dma_start(
            out=ARB[:, BO["wKv"] + 2048:BO["wKv"] + 4096].rearrange("p (kc o) -> p kc o", kc=8),
            in_=d_wv.rearrange("(kc p) o -> p kc o", p=128)), inc="qW1")
        qW1_done = cnt["qW1"]
        qW2_done = 16

        emit("P", lambda e: e.iota(T0, [[1, T]], base=0, channel_multiplier=0,
                                   allow_small_or_imprecise_dtypes=True), inc="sP")
        for ap_, val in ((one_t, 1.0), (ten4_t, 10000.0),
                         (ARF[:, LO["onesl"]:LO["onesl"] + 128], 1.0),
                         (onescol, 1.0), (ones16r, 1.0)):
            emit("P", lambda e, a=ap_, v=val: e.memset(a, v), inc="sP")
        emit("P", lambda e: e.memset(
            ARB[:, BO["v"]:BO["v"] + (T // 128) * 260], 0.0), inc="sP")
        vz = cnt["sP"]
        emit("P", lambda e: e.memset(
            ARB[:, BO["v"]:BO["v"] + (T // 128) * 260]
            .rearrange("p (j g d) -> p j g d", j=T // 128, g=4)[:, :, :, 64:65], 1.0),
            w=[("sP", vz)], inc="sP")
        sP_done = cnt["sP"]

        emit("A", lambda e: e.activation(r1(0, 1), one_t, AF.Arctan),
             w=[("sP", sP_done)], inc="sA")
        pi4_a = cnt["sA"]
        emit("A", lambda e: e.activation(r2(0, 1), ten4_t, AF.Ln), inc="sA")
        L_a = cnt["sA"]
        emit("T", lambda e: e.matmul(PSA[:, 0:1], onesl, r1(0, 1),
                                     start=True, stop=True),
             w=[("sA", pi4_a), ("sP", sP_done)], inc="sT")
        emit("T", lambda e: e.matmul(PSA[:, 512:513], onesl, r2(0, 1),
                                     start=True, stop=True),
             w=[("sA", L_a)], inc="sT")
        bc_t = cnt["sT"]
        emit("V", lambda e: e.tensor_scalar(pib, PSA[:, 0:1], 4.0, None,
                                            op0=OP.mult), w=[("sT", bc_t)])
        emit("V", lambda e: e.tensor_scalar(halfpib, PSA[:, 0:1], 2.0, None,
                                            op0=OP.mult))
        emit("V", lambda e: e.tensor_scalar(twopib, PSA[:, 0:1], 8.0, None,
                                            op0=OP.mult))
        emit("V", lambda e: e.reciprocal(inv2pib, twopib), drain=True)
        emit("V", lambda e: e.tensor_copy(Lb, PSA[:, 512:513]))
        emit("V", lambda e: e.tensor_scalar(af_v, jmod, Lb, -0.03125,
                                            op0=OP.mult, op1=OP.mult),
             w=[("qC", qC_done)], drain=True, inc="sV")
        emit("A", lambda e: e.activation(invf, af_v, AF.Exp),
             w=[("sV", cnt["sV"])], inc="sA")
        invf_a = cnt["sA"]

        STB = qf32 + 3 * T + 256
        Sb = ARBF[:, STB:STB + 256]
        Cb = ARBF[:, STB + 256:STB + 512]
        Sa = ARBF[:, STB + 512:STB + 520]
        Ca = ARBF[:, STB + 520:STB + 528]
        tmpc = ARBF[:, qf32 + T:qf32 + T + 256]
        tmpd = ARBF[:, qf32 + T + 256:qf32 + T + 512]

        def wrap_stage(W):
            t1 = T1[:, 0:W]
            t2 = T2[:, 0:W]
            emit("V", lambda e: e.tensor_scalar(t2, t1, pib, None,
                                                op0=OP.is_gt), drain=True)
            emit("V", lambda e: e.scalar_tensor_tensor(
                t1, t2, C1, t1, op0=OP.mult, op1=OP.subtract), drain=True)
            emit("V", lambda e: e.scalar_tensor_tensor(
                t1, t2, C2, t1, op0=OP.mult, op1=OP.add), drain=True)
            emit("V", lambda e: e.scalar_tensor_tensor(
                t1, t2, C3, t1, op0=OP.mult, op1=OP.add), drain=True)

        def rope_table(shift, out_slot, W, scale):
            t1 = T1[:, 0:W]
            t2 = T2[:, 0:W]
            sa0 = cnt["sA"]
            emit("V", lambda e: e.tensor_scalar(t1, T0[:, 0:W], invf, scale,
                                                op0=OP.mult, op1=OP.mult),
                 w=[("sA", sa0), ("sP", sP_done)], drain=True)
            if shift:
                emit("V", lambda e: e.tensor_scalar(t1, t1, halfpib, None,
                                                    op0=OP.add), drain=True)
            emit("V", lambda e: e.tensor_scalar(t2, t1, inv2pib, None,
                                                op0=OP.mult), drain=True)
            cw = min(64, W)
            for bb in range(W // cw):
                emit("V", lambda e, b_=bb, c_=cw: e.tensor_copy(
                    TI[:, 0:c_], T2[:, b_ * c_:b_ * c_ + c_]), drain=True)
                emit("V", lambda e, b_=bb, c_=cw: e.tensor_copy(
                    T2[:, b_ * c_:b_ * c_ + c_], TI[:, 0:c_]), drain=True)
            emit("V", lambda e: e.scalar_tensor_tensor(
                t1, t2, C1, t1, op0=OP.mult, op1=OP.subtract), drain=True)
            emit("V", lambda e: e.scalar_tensor_tensor(
                t1, t2, C2, t1, op0=OP.mult, op1=OP.add), drain=True)
            emit("V", lambda e: e.scalar_tensor_tensor(
                t1, t2, C3, t1, op0=OP.mult, op1=OP.add), drain=True)
            wrap_stage(W)
            wrap_stage(W)
            emit("V", lambda e: e.tensor_scalar(t2, t1, CLAMP, -CLAMP,
                                                op0=OP.min, op1=OP.max),
                 drain=True, inc="sV")
            vv = cnt["sV"]
            emit("A", lambda e, o=out_slot: e.activation(o, t2, AF.Sin,
                                                         scale=-1.0),
                 w=[("sV", vv)], inc="sA")
        rope_table(False, Sb, 256, 1.0)
        rope_table(True, Cb, 256, 1.0)
        rope_table(False, Sa, 8, 256.0)
        rope_table(True, Ca, 8, 256.0)
        tbl_a = cnt["sA"]
        for a in range(8):
            sl = slice(a * 256, a * 256 + 256)
            emit("V", lambda e, a_=a: e.tensor_scalar(
                tmpc, Cb, Sa[:, a_:a_ + 1], None, op0=OP.mult),
                w=[("sA", tbl_a)], drain=True)
            emit("V", lambda e, a_=a: e.tensor_scalar(
                tmpd, Sb, Ca[:, a_:a_ + 1], None, op0=OP.mult), drain=True)
            emit("V", lambda e, s=sl: e.tensor_tensor(
                sin16[:, s], tmpc, tmpd, op=OP.add), drain=True)
            emit("V", lambda e, a_=a: e.tensor_scalar(
                tmpc, Cb, Ca[:, a_:a_ + 1], None, op0=OP.mult), drain=True)
            emit("V", lambda e, a_=a: e.tensor_scalar(
                tmpd, Sb, Sa[:, a_:a_ + 1], None, op0=OP.mult), drain=True)
            emit("V", lambda e, s=sl: e.tensor_tensor(
                cos16[:, s], tmpc, tmpd, op=OP.subtract), drain=True,
                inc="sV")
        tbl_done = cnt["sV"]

        def layernorm(xn_of, prev_t, prev_v, prev_a):
            stats_hist = [prev_t, prev_t]
            stats_mm = prev_t
            for c in range(NCH):
                sl = (c % 2) * 2
                xb = actv(sl)
                xsq = actv(sl + 1)
                emit("A", lambda e, a=xb, c_=c: e.activation(
                    a, Xc(c_), AF.Copy),
                     w=[("qX", 16 * (c + 1)), ("sT", stats_hist[c % 2]),
                        ("sA", prev_a)], drain=True)
                emit("A", lambda e, a=xsq, c_=c: e.activation(
                    a, Xc(c_), AF.Square), inc="sA")
                aa_s = cnt["sA"]
                for fb in range(NFB):
                    emit("T", lambda e, f=fb, a=xb, c_=c: e.matmul(
                        PSB[0:1, f * 512:f * 512 + 512], onescol,
                        a[:, f * 512:f * 512 + 512],
                        start=(c_ == 0), stop=(c_ == NCH - 1)),
                        w=[("sA", aa_s), ("sP", sP_done), ("sV", prev_v)])
                    emit("T", lambda e, f=fb, a=xsq, c_=c: e.matmul(
                        PSA[0:1, f * 512:f * 512 + 512], onescol,
                        a[:, f * 512:f * 512 + 512],
                        start=(c_ == 0), stop=(c_ == NCH - 1)), inc="sT")
                stats_mm = cnt["sT"]
                stats_hist[c % 2] = stats_mm
            xn_done = []
            for fb in range(NFB):
                emit("V", lambda e, f=fb: e.tensor_scalar(
                    rowA, PSB[0:1, f * 512:f * 512 + 512], 1.0 / D, None,
                    op0=OP.mult),
                    w=[("sT", stats_mm), ("sT", cnt["sT"])], drain=True)
                emit("V", lambda e, f=fb: e.tensor_scalar(
                    rowB, PSA[0:1, f * 512:f * 512 + 512], 1.0 / D, 1e-5,
                    op0=OP.mult, op1=OP.add))
                emit("V", lambda e: e.tensor_tensor(rowM, rowA, rowA,
                                                    op=OP.mult), drain=True)
                emit("V", lambda e: e.tensor_tensor(rowB, rowB, rowM,
                                                    op=OP.subtract),
                    drain=True, inc="sV")
                vv = cnt["sV"]
                emit("A", lambda e: e.activation(rowE, rowB, AF.Ln),
                     w=[("sV", vv)], drain=True)
                emit("A", lambda e: e.activation(rowM, rowE, AF.Exp,
                                                 scale=-0.5),
                     drain=True, inc="sA")
                aa = cnt["sA"]
                emit("V", lambda e: e.tensor_tensor(
                    rowB, rowA, rowM, op=OP.mult),
                    w=[("sA", aa)], drain=True, inc="sV")
                vv2 = cnt["sV"]
                emit("T", lambda e, f=fb: e.matmul(
                    PSB[:, f * 512:f * 512 + 512], onesl,
                    rowM, start=True, stop=True),
                    w=[("sA", aa), ("sV", vv2)])
                emit("T", lambda e, f=fb: e.matmul(
                    PSA[:, f * 512:f * 512 + 512], onesl,
                    rowB, start=True, stop=True), inc="sT")
                tt = cnt["sT"]
                for c in range(NCH):
                    emit("V", lambda e, f=fb, c_=c: e.tensor_tensor(
                        scr1, Xc(c_, f), PSB[:, f * 512:f * 512 + 512],
                        op=OP.mult), w=[("sT", tt)], drain=True)
                    emit("V", lambda e, f=fb, c_=c, xo=xn_of: e.tensor_tensor(
                        ARB[:, xo + c_ * T + f * 512:xo + c_ * T + f * 512 + 512],
                        scr1, PSA[:, f * 512:f * 512 + 512], op=OP.subtract),
                        drain=True, inc="sV")
                xn_done.append(cnt["sV"])
            return xn_done, cnt["sV"], stats_mm

        xn_fb, ln1_v, _ = layernorm(BO["act"], bc_t, cnt["sV"], tbl_a)

        flr = {"acc": ln1_v,
               "rot": ln1_v,
               "qpre_t": 0,
               "qpre_p": 0,
               "rsc0": 0,
               "rsc1": cnt["sP"]}
        qdone = {}
        mm_tail = {"t": 0}

        def tile_mm(wfn, col, fb, kc, nco_w):
            w = []
            if kc == 0:
                w = [("sV", flr["acc"]), ("sV", xn_fb[fb]), ("qW1", qW1_done)]
            emit("T", lambda e, k=kc, c_=col, f=fb, wf=wfn: e.matmul(
                FACC, wf(k, c_), actv(k, f),
                start=(k == 0), stop=(k == NCH - 1)), w=w,
                inc=("sT" if kc == NCH - 1 else None))
            if kc == NCH - 1:
                mm_tail["t"] = cnt["sT"]

        def tile_epi1(bias_ap, col):
            mm_t = mm_tail["t"]
            emit("V", lambda e, b=bias_ap, c_=col: e.tensor_scalar(
                qpre, FACC, b[:, c_:c_ + 1], None, op0=OP.add),
                w=[("sT", mm_t), ("sT", flr["qpre_t"]),
                   ("sP", flr["qpre_p"])], drain=True, inc="sV")
            flr["acc"] = cnt["sV"]

        def tile_epi2(fb, out_of, tag):
            emit("T", lambda e: e.matmul(FROT, p128t, qpre,
                                         start=True, stop=True),
                 w=[("sV", flr["acc"]), ("sV", flr["rot"]),
                    ("qC", qC_done)], inc="sT")
            flr["qpre_t"] = cnt["sT"]
            rot_t = cnt["sT"]
            emit("P", lambda e, f=fb: e.tensor_tensor(
                qpre, qpre, cos16[:, f * 512:f * 512 + 512], op=OP.mult),
                w=[("sV", flr["acc"]), ("sT", rot_t), ("sV", tbl_done)],
                drain=True, inc="sP")
            emit("V", lambda e, f=fb: e.tensor_tensor(
                rsc1, FROT, sin16[:, f * 512:f * 512 + 512], op=OP.mult),
                w=[("sT", rot_t), ("sP", flr["rsc1"]), ("sV", tbl_done)],
                drain=True, inc="sV")
            flr["rot"] = cnt["sV"]
            emit("P", lambda e, oo=out_of, f=fb: e.tensor_tensor(
                ARB[:, oo + f * 512:oo + f * 512 + 512],
                qpre, rsc1, op=OP.add),
                w=[("sV", flr["rot"])], drain=True, inc="sP")
            flr["qpre_p"] = cnt["sP"]
            flr["rsc1"] = cnt["sP"]
            qdone[(tag, fb)] = cnt["sP"]

        for fb in range(NFB):
            for kc in range(NCH):
                tile_mm(wKc, 0, fb, kc, 2)
            tile_epi1(bk_t, 0)
            tile_epi2(fb, BO["k"], ("k", 0))
        for fb in range(NFB):
            for kc in range(NCH):
                tile_mm(wA, 0, fb, kc, NCH)
            tile_epi1(bq_t, 0)
            tile_epi2(fb, BO["q"], 0)

        vb_pp = [ln1_v, ln1_v]
        for j in range(T // 128):
            pp = j % 2
            pso = PSA[:, 1024 + pp * 512:1024 + pp * 512 + 256]
            for kc in range(NCH):
                w = [("sV", xn_fb[(j * 128) // 512]), ("sV", vb_pp[pp]),
                     ("qW1", qW1_done)] if kc == 0 else []
                emit("T", lambda e, k=kc, j_=j, ps=pso: e.matmul(
                    ps, actv(k)[:, j_ * 128:j_ * 128 + 128], wVc(k),
                    start=(k == 0), stop=False), w=w)
            emit("T", lambda e, ps=pso: e.matmul(
                ps, onesl, bvrow,
                start=False, stop=True), w=[("qC", qC_done)], inc="sT")
            tt = cnt["sT"]
            emit("V", lambda e, j_=j, ps=pso: e.tensor_copy(
                ARB[:, BO["v"] + j_ * 260:BO["v"] + j_ * 260 + 260]
                .rearrange("p (g d) -> p g d", g=4)[:, :, 0:64],
                ps.rearrange("p (g d) -> p g d", g=4)),
                w=[("sT", tt), ("sP", sP_done)], drain=True, inc="sV")
            vb_pp[pp] = cnt["sV"]
        v_done = cnt["sV"]
        for fb in range(NFB):
            for kc in range(NCH):
                tile_mm(wKc, 1, fb, kc, 2)
            tile_epi1(bk_t, 1)
            tile_epi2(fb, BO["k"] + T, ("k", 1))
        k_done = {0: max(qdone[(("k", 0), f)] for f in range(NFB)),
                  1: max(qdone[(("k", c), f)] for c in range(2)
                         for f in range(NFB))}
        kv_mm_done = cnt["sT"]

        emit("P", lambda e: e.dma_start(
            out=ARB[:, BO["wR1"]:BO["wR1"] + 8192].rearrange("p (kc o) -> p kc o", kc=8),
            in_=d_wo.rearrange("(kc p) o -> p kc o", p=128)),
            w=[("sT", cnt["sT"])], inc="qW2")
        assert cnt["qW2"] == qW2_done

        last_exp_psa = 0
        sc_guard = [tbl_a, tbl_a]
        slot_pv = [kv_mm_done] * 4
        pv_evac = [v_done, v_done]
        tail_bB = [0]
        evg = [0, 0]
        pending_tail = [None]
        pending_epi = [None]
        attn_mm_last = 0
        q_filler_last = kv_mm_done
        first_unit = [True]
        rowABs = ARF[32:33, LO["scr1"]:LO["scr1"] + 1024]

        def sc_head(g, hx, c_, q_, ck_):
            ba = 64 * hx
            for jj in range(2):
                j = 2 * g + jj
                w = []
                if jj == 0:
                    w = [("sA", sc_guard[hx])]
                    if g == 0:
                        w += [("sP", qdone[(c_, q_)]),
                              ("sP", k_done[ck_]), ("sV", v_done)]
                        if first_unit[0]:
                            w += [("sV", ln1_v)]
                emit("T", lambda e, j_=j, jj_=jj, hx_=hx, ba_=ba:
                     e.matmul(
                    PSA[:, hx_ * 1024 + jj_ * 512:
                        hx_ * 1024 + jj_ * 512 + 512],
                    kv_(ck_, ba_, j_),
                    qv(c_, q_, ba_, 64), start=True, stop=True),
                    w=w,
                    inc=("sT" if jj == 1 else None))
            first_unit[0] = False
            return cnt["sT"]

        def pv_head(g, hx, exp_a, ck_):
            sl = 2 * hx + (g % 2)
            for jj in range(2):
                j = 2 * g + jj
                w = [("sA", exp_a)]
                if g == 0 and jj == 0:
                    w += [("sV", pv_evac[hx]), ("sP", sP_done)]
                emit("T", lambda e, j_=j, s=sl, jj_=jj, hx_=hx, k2=ck_:
                     e.matmul(
                    PSB[:, hx_ * 512:hx_ * 512 + 512],
                    vones(j_, 2 * k2 + hx_),
                    ARB[:, BO["esl"] + s * 1024 + jj_ * 512:
                        BO["esl"] + s * 1024 + jj_ * 512 + 512],
                    start=(j_ == 0), stop=(j_ == NJ - 1)), w=w,
                    inc=("sT" if jj == 1 else None))
            slot_pv[sl] = cnt["sT"]

        units = [(pr, qb) for pr in range(NCH) for qb in range(NQB)]
        hoisted = {}
        for ui, (pr, qb) in enumerate(units):
            ck = 0 if pr < 4 else 1
            fc = pr + 1
            par = ui % 2
            if pending_tail[0] is not None:
                pending_tail[0][0]()
            if (pr, qb) in hoisted:
                sc_t_next = hoisted.pop((pr, qb))
            else:
                sc_t_next = [sc_head(0, 0, pr, qb, ck),
                             sc_head(0, 1, pr, qb, ck)]
            if pending_epi[0] is not None:
                pending_epi[0][0]()
            for g in range(NG2):
                sc_t = sc_t_next
                exp_as = []
                for hx in range(2):
                    sl = 2 * hx + (g % 2)
                    emit("A", lambda e, s=sl, hx_=hx: e.activation(
                        ARB[:, BO["esl"] + s * 1024:
                            BO["esl"] + s * 1024 + 1024],
                        PSA[:, hx_ * 1024:hx_ * 1024 + 1024],
                        AF.Exp, scale=0.125),
                        w=[("sT", sc_t[hx]), ("sT", slot_pv[sl])],
                        inc="sA")
                    exp_as.append(cnt["sA"])
                    sc_guard[hx] = cnt["sA"]
                    last_exp_psa = cnt["sA"]
                if g == 1 and pending_tail[0] is not None:
                    pending_tail[0][1]()
                nxt = [0, 0]
                if g + 1 < NG2:
                    nxt[0] = sc_head(g + 1, 0, pr, qb, ck)
                    nxt[1] = sc_head(g + 1, 1, pr, qb, ck)
                elif ui + 1 < len(units):
                    pr2, qb2 = units[ui + 1]
                    ck2 = 0 if pr2 < 4 else 1
                    hoisted[(pr2, qb2)] = [
                        sc_head(0, 0, pr2, qb2, ck2),
                        sc_head(0, 1, pr2, qb2, ck2)]
                pv_head(g, 0, exp_as[0], ck)
                pv_tA = cnt["sT"]
                pv_head(g, 1, exp_as[1], ck)
                sc_t_next = nxt
                if fc < NCH:
                    tile_mm(wA, fc, qb, g, NCH)
                if g == 0 and pending_epi[0] is not None:
                    pending_epi[0][1]()
                    pending_epi[0] = None
                if g == 2 and pending_tail[0] is not None:
                    pending_tail[0][2]()
                if g == 3 and pending_tail[0] is not None:
                    pending_tail[0][3]()
                    pending_tail[0] = None
            pv_t = cnt["sT"]
            attn_mm_last = pv_t
            if fc < NCH:
                q_filler_last = mm_tail["t"]
                pending_epi[0] = (
                    lambda f_=fc: tile_epi1(bq_t, f_),
                    lambda f_=fc, q_=qb: tile_epi2(q_, BO["q"] + f_ * T, f_))

            tail_rc = [0]
            tail_cA = [0]
            tail_ev = [0]

            def tailEv(ptA=pv_tA, pt=pv_t, EV=EVb[par], par_=par):
                emit("V", lambda e: e.tensor_copy(
                    EV[:, 0:512], PSB[:, 0:512]),
                    w=[("sT", ptA), ("sV", evg[par_])],
                    drain=True, inc="sV")
                pv_evac[0] = cnt["sV"]
                emit("V", lambda e: e.tensor_copy(
                    EV[:, 512:1024], PSB[:, 512:1024]),
                    w=[("sT", pt)], drain=True, inc="sV")
                pv_evac[1] = cnt["sV"]
                tail_ev[0] = cnt["sV"]

            def tailD(EV=EVb[par]):
                emit("A", lambda e, r_=rowABs: e.activation(
                    r_, EV[64:65, :], AF.Ln), w=[("sV", tail_ev[0])])
                emit("A", lambda e, r_=rowABs: e.activation(
                    rowAB16, r_, AF.Exp, scale=-1.0),
                    w=[("sT", tail_bB[0])], drain=True, inc="sA")
                tail_rc[0] = cnt["sA"]

            def tailA(c_=pr, q_=qb, EV=EVb[par]):
                emit("T", lambda e: e.matmul(
                    FROT, ones16r, rowAB16[:, 0:512],
                    start=True, stop=True),
                    w=[("sA", tail_rc[0]), ("sV", flr["rot"])],
                    inc="sT")
                bA = cnt["sT"]
                emit("V", lambda e: e.tensor_copy(scr2, FROT),
                    w=[("sT", bA)], drain=True, inc="sV")
                tail_cA[0] = cnt["sV"]
                emit("V", lambda e: e.tensor_tensor(
                    qv(c_, q_, 0, 64), EV[0:64, 0:512],
                    scr2[0:64, :], op=OP.mult), drain=True, inc="sV")

            def tailB(c_=pr, q_=qb, EV=EVb[par], par_=par):
                emit("T", lambda e: e.matmul(
                    FROT, ones16r, rowAB16[:, 512:1024],
                    start=True, stop=True),
                    w=[("sV", tail_cA[0])], inc="sT")
                tail_bB[0] = cnt["sT"]
                emit("V", lambda e: e.tensor_copy(scr2, FROT),
                    w=[("sT", tail_bB[0])], drain=True, inc="sV")
                emit("V", lambda e: e.tensor_tensor(
                    qv(c_, q_, 64, 64), EV[0:64, 512:1024],
                    scr2[0:64, :], op=OP.mult), drain=True, inc="sV")
                evg[par_] = cnt["sV"]
                flr["rot"] = cnt["sV"]
            pending_tail[0] = (tailEv, tailD, tailA, tailB)
        if pending_epi[0] is not None:
            for fn in pending_epi[0]:
                fn()
            pending_epi[0] = None
        if pending_tail[0] is not None:
            for fn in pending_tail[0]:
                fn()
            pending_tail[0] = None
        ctx_done = cnt["sV"]

        emit("P", lambda e: e.dma_start(
            out=ARB[:, BO["wA"]:BO["wA"] + 8192].rearrange("p (kc o) -> p kc o", kc=8),
            in_=d_w1.rearrange("(kc p) o -> p kc o", p=128)),
            w=[("sT", q_filler_last)], inc="qW3")
        qW3_done = cnt["qW3"]
        emit("P", lambda e: e.dma_start(
            out=ARB[:, BO["w3"]:BO["w3"] + 8192].rearrange("p (kc o) -> p kc o", kc=8),
            in_=d_w3.rearrange("(kc p) o -> p kc o", p=128)),
            w=[("sT", attn_mm_last)], inc="qW5")
        qW5_done = cnt["qW5"]

        pp_x = [ctx_done, ctx_done]
        oproj_mm = 0
        for c in range(NCH):
            for fb in range(NFB):
                pp = (c * NFB + fb) % 2
                for kc in range(NCH):
                    w = [("qW2", qW2_done), ("sV", ctx_done),
                         ("sV", pp_x[pp]), ("sA", last_exp_psa)] \
                        if kc == 0 else []
                    emit("T", lambda e, k=kc, c_=c, f=fb, p=pp: e.matmul(
                        psA(p), wR1(k, c_), qv(k, f),
                        start=(k == 0), stop=(k == NCH - 1)), w=w,
                        inc=("sT" if kc == NCH - 1 else None))
                tt = cnt["sT"]
                oproj_mm = tt
                emit("V", lambda e, p=pp, c_=c, f=fb: e.scalar_tensor_tensor(
                    Xc(c_, f), psA(p), bo_t[:, c_:c_ + 1], Xc(c_, f),
                    op0=OP.add, op1=OP.add), w=[("sT", tt)], drain=True,
                    inc="sV")
                pp_x[pp] = cnt["sV"]
        x_attn = cnt["sV"]

        emit("P", lambda e: e.dma_start(
            out=ARB[:, BO["wR1"]:BO["wR1"] + 8192].rearrange("p (kc o) -> p kc o", kc=8),
            in_=d_w2.rearrange("(kc p) o -> p kc o", p=128)),
            w=[("sT", oproj_mm)], inc="qW4")
        qW4_done = cnt["qW4"]

        xn2_fb, ln2_v, ln2_mm = layernorm(BO["act"], oproj_mm, x_attn,
                                          last_exp_psa)

        def ffn_mm(wfn, rhs_of, out_of, wsem, wval, rhs_v, bank_guard):
            pp_g = [bank_guard, bank_guard]
            last = 0
            for c in range(NCH):
                for fb in range(NFB):
                    pp = (c * NFB + fb) % 2
                    for kc in range(NCH):
                        w = [(wsem, wval), ("sV", rhs_v), ("sV", pp_g[pp])] \
                            if kc == 0 else []
                        emit("T", lambda e, k=kc, c_=c, f=fb, p=pp, ro=rhs_of:
                             e.matmul(psA(p), wfn(k, c_),
                                      ARB[:, ro + k * T + f * 512:
                                          ro + k * T + f * 512 + 512],
                                      start=(k == 0), stop=(k == NCH - 1)),
                             w=w, inc=("sT" if kc == NCH - 1 else None))
                    tt = cnt["sT"]
                    last = tt
                    emit("V", lambda e, p=pp, c_=c, f=fb, oo=out_of:
                         e.tensor_copy(
                             ARB[:, oo + c_ * T + f * 512:
                                 oo + c_ * T + f * 512 + 512], psA(p)),
                         w=[("sT", tt)], drain=True, inc="sV")
                    pp_g[pp] = cnt["sV"]
            return last, cnt["sV"]

        f1_mm, h1_v = ffn_mm(wA, BO["act"], BO["q"], "qW3", qW3_done,
                             ln2_v, ln2_v)
        f2_mm, h2_v = ffn_mm(wR1, BO["q"], BO["act"], "qW4", qW4_done,
                             h1_v, h1_v)
        psa_guard_a = [0, 0, 0, 0]
        tsc_guard = [h2_v, h2_v]
        for c in range(NCH):
            for fb in range(NFB):
                idx = c * NFB + fb
                pp = idx % 4
                ts_ = scr1 if idx % 2 == 0 else scr2
                for kc in range(NCH):
                    w = [("qW5", qW5_done), ("sV", h2_v),
                         ("sA", psa_guard_a[pp])] \
                        if kc == 0 else []
                    emit("T", lambda e, k=kc, c_=c, f=fb, p=pp: e.matmul(
                        psA(p), wKV3(k, c_), actv(k, f),
                        start=(k == 0), stop=(k == NCH - 1)), w=w,
                        inc=("sT" if kc == NCH - 1 else None))
                tt = cnt["sT"]
                emit("A", lambda e, p=pp, c_=c, t_=ts_: e.activation(
                    t_, psA(p), AF.Tanh, bias=bfn_t[:, c_:c_ + 1]),
                    w=[("sT", tt), ("sV", tsc_guard[idx % 2])],
                    drain=True, inc="sA")
                aa = cnt["sA"]
                psa_guard_a[pp] = aa
                emit("V", lambda e, c_=c, f=fb, t_=ts_: e.tensor_tensor(
                    Xc(c_, f), Xc(c_, f), t_, op=OP.add),
                    w=[("sA", aa)], drain=True, inc="sV")
                tsc_guard[idx % 2] = cnt["sV"]
            emit("S", lambda e, c_=c: e.dma_start(
                out=d_y[c_ * 128:c_ * 128 + 128, :], in_=Xc(c_)),
                w=[("sV", cnt["sV"])], inc="qO")
        emit("S", lambda e: None, w=[("qO", cnt["qO"])])

        import contextlib
        with contextlib.ExitStack() as stack:
            sems = {s: stack.enter_context(nc.semaphore(s)) for s in SEMS}
            with nc.Block() as blk:
                def replay(key):
                    def body(eng):
                        for drain, ws, fn, inc in prog[key]:
                            if drain:
                                eng.drain(fusable=True)
                            for (s, v) in ws:
                                eng.wait_ge(sems[s], v)
                            ins = fn(eng)
                            if inc is not None:
                                ins.then_inc(sems[inc],
                                             16 if inc in DMA_SEMS else 1)
                    return body
                build._last = (prog, dict(cnt))
                blk.tensor(replay("T"))
                blk.vector(replay("V"))
                blk.scalar(replay("A"))
                blk.gpsimd(replay("P"))
                blk.sync(replay("S"))
    return nc


def _stage_consts(inputs):
    bf = ml_dtypes.bfloat16
    wq = np.asarray(inputs["L0_attn_Wq"], dtype=np.float32)
    wk = np.asarray(inputs["L0_attn_Wk"], dtype=np.float32)
    wv = np.asarray(inputs["L0_attn_Wv"], dtype=np.float32)
    wo = np.asarray(inputs["L0_attn_Wo"], dtype=np.float32)
    bq = np.asarray(inputs["L0_attn_bq"], dtype=np.float32)
    bk = np.asarray(inputs["L0_attn_bk"], dtype=np.float32)
    bv = np.asarray(inputs["L0_attn_bv"], dtype=np.float32)
    bo = np.asarray(inputs["L0_attn_bo"], dtype=np.float32)
    w1 = np.asarray(inputs["L0_ffn_weight"], dtype=np.float32)
    w2 = np.asarray(inputs["L0_ffn_aux_weight1"], dtype=np.float32)
    w3 = np.asarray(inputs["L0_ffn_aux_weight2"], dtype=np.float32)
    bfn = np.asarray(inputs["L0_ffn_bias"], dtype=np.float32)

    shared = {
        "wqT": np.ascontiguousarray(wq.T[:, SIGMA]),
        "wkT": np.ascontiguousarray(wk.T),
        "wvT": np.ascontiguousarray(wv.T),
        "woT": np.ascontiguousarray(wo.T[SIGMA, :]),
        "w1": np.ascontiguousarray(w1),
        "w2": np.ascontiguousarray(w2),
        "w3": np.ascontiguousarray(w3),
        "bq_t": np.ascontiguousarray(bq[SIGMA].reshape(8, 128).T),
        "bk_t": np.ascontiguousarray(bk.reshape(2, 128).T),
        "bo_t": np.ascontiguousarray(bo.reshape(8, 128).T),
        "bfn_t": np.ascontiguousarray(bfn.reshape(8, 128).T),
        "bv_row": np.ascontiguousarray(bv.reshape(1, 256)),
        "jmod": (np.arange(128, dtype=np.float32) % 32).reshape(128, 1),
    }
    p128 = np.zeros((128, 128), dtype=np.float32)
    for b in (0, 64):
        for i in range(32):
            p128[b + i, b + 2 * i + 1] = -1.0
            p128[b + 32 + i, b + 2 * i] = 1.0
    shared["p128t"] = np.ascontiguousarray(p128.T).astype(bf)
    return shared


def kernel(**inputs):
    T = 2048
    x = np.asarray(inputs["L0_attn_hidden_states"], dtype=np.float32)
    shared = _stage_consts(inputs)
    nc = build(T)
    in_maps = []
    for i in range(N_CORES):
        m = dict(shared)
        m["xT"] = np.ascontiguousarray(x[i].T)
        in_maps.append(m)
    res = run_bass_kernel_spmd(nc, in_maps, core_ids=list(range(N_CORES)))
    out = np.stack([np.asarray(r["yT"]).T for r in res.results], axis=0)
    return np.ascontiguousarray(out, dtype=np.float32)


# version 38
# speedup vs baseline: 1.0356x; 1.0356x over previous
import numpy as np
import ml_dtypes
import concourse.bass as bass
import concourse.mybir as mybir
from concourse.bass_utils import run_bass_kernel_spmd

F32 = mybir.dt.float32
BF16 = mybir.dt.bfloat16
F32R = mybir.dt.float32r
AF = mybir.ActivationFunctionType
OP = mybir.AluOpType

D = 1024
NCH = 8
H = 16
NKV = 4
N_CORES = 8
A_H = [0, 1, 2, 3, 8, 9, 10, 11]
B_H = [4, 5, 6, 7, 12, 13, 14, 15]
SIGMA = np.concatenate(
    [np.r_[64 * A_H[c]:64 * A_H[c] + 64, 64 * B_H[c]:64 * B_H[c] + 64]
     for c in range(NCH)])
CLAMP = 3.141592
_TP = 2.0 * np.pi
C1 = float(np.float32(_TP))
C2 = float(np.float32(_TP - C1))
C3 = float(np.float32(_TP - C1 - C2))


def f32_layout(T):
    o = {}
    o["X"] = 0
    o["cos"] = NCH * T
    o["sin"] = NCH * T + T // 2
    o["scr1"] = NCH * T + T
    o["scr2"] = o["scr1"] + 512
    o["ev"] = o["scr2"] + 512
    base = o["ev"] + 2048
    o["consts"] = base
    o["bq"] = base + 8
    o["bo"] = base + 16
    o["bfn"] = base + 24
    o["bk"] = base + 32
    o["jmod"] = base + 34
    o["one"] = base + 35
    o["ten4"] = base + 36
    o["onesl"] = base + 40
    o["end"] = base + 168
    return o


def bf16_layout(T):
    o = {}
    o["act"] = 0
    o["q"] = NCH * T
    o["k"] = 2 * NCH * T
    o["v"] = 2 * NCH * T + 2 * T
    o["wA"] = o["v"] + (T // 128) * 260
    o["wKv"] = o["wA"] + 8192
    o["esl"] = o["wKv"]
    o["wR1"] = o["wKv"] + 4096
    o["p128t"] = o["wR1"] + 8192
    o["qpre"] = o["p128t"] + 128
    o["onescol"] = o["qpre"] + 512
    o["rsc"] = o["onescol"] + 8
    o["ones16r"] = o["rsc"] + 512
    o["end"] = o["ones16r"] + 128
    if 2 * T + (T // 128) * 260 >= 8192:
        o["w3"] = o["k"]
    else:
        o["w3"] = o["end"]
        o["end"] += 8192
    return o


def build(T, h_limit=H):
    NFB = T // 512
    NJ = T // 128
    NQB = T // 512
    NG2 = NJ // 2
    LO = f32_layout(T)
    BO = bf16_layout(T)

    nc = bass.Bass("TRN2", target_bir_lowering=False)
    d_x = nc.dram_tensor("xT", [D, T], F32, kind="ExternalInput").ap()
    d_wq = nc.dram_tensor("wqT", [D, D], F32, kind="ExternalInput").ap()
    d_wk = nc.dram_tensor("wkT", [D, 256], F32, kind="ExternalInput").ap()
    d_wv = nc.dram_tensor("wvT", [D, 256], F32, kind="ExternalInput").ap()
    d_wo = nc.dram_tensor("woT", [D, D], F32, kind="ExternalInput").ap()
    d_w1 = nc.dram_tensor("w1", [D, D], F32, kind="ExternalInput").ap()
    d_w2 = nc.dram_tensor("w2", [D, D], F32, kind="ExternalInput").ap()
    d_w3 = nc.dram_tensor("w3", [D, D], F32, kind="ExternalInput").ap()
    d_bq = nc.dram_tensor("bq_t", [128, 8], F32, kind="ExternalInput").ap()
    d_bk = nc.dram_tensor("bk_t", [128, 2], F32, kind="ExternalInput").ap()
    d_bo = nc.dram_tensor("bo_t", [128, 8], F32, kind="ExternalInput").ap()
    d_bfn = nc.dram_tensor("bfn_t", [128, 8], F32, kind="ExternalInput").ap()
    d_bv = nc.dram_tensor("bv_row", [1, 256], F32, kind="ExternalInput").ap()
    d_jmod = nc.dram_tensor("jmod", [128, 1], F32, kind="ExternalInput").ap()
    d_p128t = nc.dram_tensor("p128t", [128, 128], BF16, kind="ExternalInput").ap()
    d_y = nc.dram_tensor("yT", [D, T], F32, kind="ExternalOutput").ap()

    SEMS = ["qX", "qC", "qW1", "qW2", "qW3", "qW4", "qW5", "qO",
            "sT", "sV", "sA", "sP"]
    DMA_SEMS = {"qX", "qC", "qW1", "qW2", "qW3", "qW4", "qW5", "qO"}
    cnt = {s: 0 for s in SEMS}
    prog = {e: [] for e in "TVAPS"}
    waited = {e: {s: 0 for s in SEMS} for e in "TVAPS"}

    def emit(eng, fn, w=(), inc=None, drain=False):
        ws = []
        for (s, v) in w:
            if v > waited[eng][s]:
                ws.append((s, v))
                waited[eng][s] = v
        if inc is not None:
            cnt[inc] += 16 if inc in DMA_SEMS else 1
        prog[eng].append((drain, tuple(ws), fn, inc))
        return cnt[inc] if inc is not None else None

    with (
        nc.sbuf_tensor("ARF", [128, LO["end"]], F32) as ARF,
        nc.sbuf_tensor("ARB", [128, BO["end"]], BF16) as ARB,
        nc.sbuf_tensor("TI", [128, 64], mybir.dt.int32) as TI,
        nc.psum_tensor("PSA", [128, 2048], F32) as PSA,
        nc.psum_tensor("PSB", [128, 2048], F32) as PSB,
    ):
        def Xc(c, fb=None):
            o = LO["X"] + c * T
            if fb is None:
                return ARF[:, o:o + T]
            return ARF[:, o + fb * 512:o + fb * 512 + 512]
        cos16 = ARF[:, LO["cos"]:LO["cos"] + T // 2].bitcast(BF16)[:, 0:T]
        sin16 = ARF[:, LO["sin"]:LO["sin"] + T // 2].bitcast(BF16)[:, 0:T]
        EVb = [ARF[:, LO["ev"] + p * 1024:LO["ev"] + p * 1024 + 1024]
               for p in (0, 1)]
        scr1 = ARF[:, LO["scr1"]:LO["scr1"] + 512]
        scr2 = ARF[:, LO["scr2"]:LO["scr2"] + 512]
        def r1(p, n=512):
            return ARF[p:p + 1, LO["scr1"]:LO["scr1"] + n]
        def r2(p, n=512):
            return ARF[p:p + 1, LO["scr2"]:LO["scr2"] + n]
        rowA = ARF[0:1, LO["scr1"]:LO["scr1"] + 512]
        rowB = ARF[0:1, LO["scr2"]:LO["scr2"] + 512]
        rowE = ARF[32:33, LO["scr2"]:LO["scr2"] + 512]
        def cvec(i):
            return ARF[:, LO["consts"] + i:LO["consts"] + i + 1]
        pib, halfpib, twopib, Lb, invf, af_v, inv2pib = (cvec(i) for i in range(7))
        bq_t = ARF[:, LO["bq"]:LO["bq"] + 8]
        bo_t = ARF[:, LO["bo"]:LO["bo"] + 8]
        bfn_t = ARF[:, LO["bfn"]:LO["bfn"] + 8]
        bk_t = ARF[:, LO["bk"]:LO["bk"] + 2]
        jmod = ARF[:, LO["jmod"]:LO["jmod"] + 1]
        one_t = ARF[0:1, LO["one"]:LO["one"] + 1]
        ten4_t = ARF[0:1, LO["ten4"]:LO["ten4"] + 1]
        def onesF(p):
            return ARF[p:p + 1, LO["onesl"]:LO["onesl"] + 128]
        onesl = onesF(0)

        ARBF = ARB[:].bitcast(F32)
        def actv(c, fb=None):
            o = BO["act"] + c * T
            if fb is None:
                return ARB[:, o:o + T]
            return ARB[:, o + fb * 512:o + fb * 512 + 512]
        def eslot(sl, jj=None):
            o = BO["esl"] + sl * 1024
            if jj is None:
                return ARB[:, o:o + 1024]
            return ARB[:, o + jj * 512:o + jj * 512 + 512]
        def qv(c, fb=None, base=0, np_=128):
            o = BO["q"] + c * T
            if fb is None:
                return ARB[base:base + np_, o:o + T]
            return ARB[base:base + np_, o + fb * 512:o + fb * 512 + 512]
        def kv_(ck, base, j):
            o = BO["k"] + ck * T + j * 128
            return ARB[base:base + 64, o:o + 128]
        def vones(j, g):
            o = BO["v"] + j * 260 + 65 * g
            return ARB[:, o:o + 128]
        def wA(kc, co):
            o = BO["wA"] + kc * D + co * 128
            return ARB[:, o:o + 128]
        def wKc(kc, co):
            o = BO["wKv"] + kc * 256 + co * 128
            return ARB[:, o:o + 128]
        def wVc(kc):
            o = BO["wKv"] + 2048 + kc * 256
            return ARB[:, o:o + 256]
        def wR1(kc, co):
            o = BO["wR1"] + kc * D + co * 128
            return ARB[:, o:o + 128]
        def wKV3(kc, co):
            o = BO["w3"] + kc * D + co * 128
            return ARB[:, o:o + 128]
        p128t = ARB[:, BO["p128t"]:BO["p128t"] + 128]
        qpre = ARB[:, BO["qpre"]:BO["qpre"] + 512]
        onescol = ARB[:, BO["onescol"]:BO["onescol"] + 1]
        rsc1 = ARB[:, BO["rsc"]:BO["rsc"] + 512]
        ones16r = ARB[0:1, BO["ones16r"]:BO["ones16r"] + 128]
        _lnr = 2 * (BO["q"] // 2 + 3 * T + 256 + 256)
        rowM16 = ARB[0:1, _lnr:_lnr + 512]
        rowNB16 = ARB[0:1, _lnr + 512:_lnr + 1024]
        rowAB16 = ARF[:, LO["scr1"]:LO["scr1"] + 512].bitcast(BF16)[0:1, 0:1024]
        qf32 = BO["q"] // 2
        _rm = (BO["q"] + NCH * T) // 2 - 1024
        rowM = ARBF[0:1, _rm:_rm + 512]
        bvrow = ARBF[0:1, qf32 + 3 * T:qf32 + 3 * T + 256]
        T0 = ARBF[:, qf32:qf32 + T]
        T1 = ARBF[:, qf32 + T:qf32 + 2 * T]
        T2 = ARBF[:, qf32 + 2 * T:qf32 + 3 * T]
        def psA(b, n=512):
            return PSA[:, b * 512:b * 512 + n]
        def psB(b, n=512):
            return PSB[:, b * 512:b * 512 + n]
        FACC = PSB[:, 1024:1536]
        FROT = PSB[:, 1536:2048]

        for dst, src in ((bq_t, d_bq), (bk_t, d_bk), (bo_t, d_bo),
                         (bfn_t, d_bfn), (bvrow, d_bv), (jmod, d_jmod),
                         (p128t, d_p128t)):
            emit("S", lambda e, a=dst, b=src: e.dma_start(out=a, in_=b), inc="qC")
        qC_done = cnt["qC"]
        for c in range(NCH):
            emit("S", lambda e, c_=c: e.dma_start(
                out=Xc(c_), in_=d_x[c_ * 128:c_ * 128 + 128, :]), inc="qX")
        emit("P", lambda e: e.dma_start(
            out=ARB[:, BO["wA"]:BO["wA"] + 8192].rearrange("p (kc o) -> p kc o", kc=8),
            in_=d_wq.rearrange("(kc p) o -> p kc o", p=128)), inc="qW1")
        emit("P", lambda e: e.dma_start(
            out=ARB[:, BO["wKv"]:BO["wKv"] + 2048].rearrange("p (kc o) -> p kc o", kc=8),
            in_=d_wk.rearrange("(kc p) o -> p kc o", p=128)), inc="qW1")
        emit("P", lambda e: e.dma_start(
            out=ARB[:, BO["wKv"] + 2048:BO["wKv"] + 4096].rearrange("p (kc o) -> p kc o", kc=8),
            in_=d_wv.rearrange("(kc p) o -> p kc o", p=128)), inc="qW1")
        qW1_done = cnt["qW1"]
        qW2_done = 16

        emit("P", lambda e: e.iota(T0, [[1, T]], base=0, channel_multiplier=0,
                                   allow_small_or_imprecise_dtypes=True), inc="sP")
        for ap_, val in ((one_t, 1.0), (ten4_t, 10000.0),
                         (ARF[:, LO["onesl"]:LO["onesl"] + 128], 1.0),
                         (onescol, 1.0), (ones16r, 1.0)):
            emit("P", lambda e, a=ap_, v=val: e.memset(a, v), inc="sP")
        emit("P", lambda e: e.memset(
            ARB[:, BO["v"]:BO["v"] + (T // 128) * 260], 0.0), inc="sP")
        vz = cnt["sP"]
        emit("P", lambda e: e.memset(
            ARB[:, BO["v"]:BO["v"] + (T // 128) * 260]
            .rearrange("p (j g d) -> p j g d", j=T // 128, g=4)[:, :, :, 64:65], 1.0),
            w=[("sP", vz)], inc="sP")
        sP_done = cnt["sP"]

        emit("A", lambda e: e.activation(r1(0, 1), one_t, AF.Arctan),
             w=[("sP", sP_done)], inc="sA")
        pi4_a = cnt["sA"]
        emit("A", lambda e: e.activation(r2(0, 1), ten4_t, AF.Ln), inc="sA")
        L_a = cnt["sA"]
        emit("T", lambda e: e.matmul(PSA[:, 0:1], onesl, r1(0, 1),
                                     start=True, stop=True),
             w=[("sA", pi4_a), ("sP", sP_done)], inc="sT")
        emit("T", lambda e: e.matmul(PSA[:, 512:513], onesl, r2(0, 1),
                                     start=True, stop=True),
             w=[("sA", L_a)], inc="sT")
        bc_t = cnt["sT"]
        emit("V", lambda e: e.tensor_scalar(pib, PSA[:, 0:1], 4.0, None,
                                            op0=OP.mult), w=[("sT", bc_t)])
        emit("V", lambda e: e.tensor_scalar(halfpib, PSA[:, 0:1], 2.0, None,
                                            op0=OP.mult))
        emit("V", lambda e: e.tensor_scalar(twopib, PSA[:, 0:1], 8.0, None,
                                            op0=OP.mult))
        emit("V", lambda e: e.reciprocal(inv2pib, twopib), drain=True)
        emit("V", lambda e: e.tensor_copy(Lb, PSA[:, 512:513]))
        emit("V", lambda e: e.tensor_scalar(af_v, jmod, Lb, -0.03125,
                                            op0=OP.mult, op1=OP.mult),
             w=[("qC", qC_done)], drain=True, inc="sV")
        emit("A", lambda e: e.activation(invf, af_v, AF.Exp),
             w=[("sV", cnt["sV"])], inc="sA")
        invf_a = cnt["sA"]

        STB = qf32 + 3 * T + 256
        Sb = ARBF[:, STB:STB + 256]
        Cb = ARBF[:, STB + 256:STB + 512]
        Sa = ARBF[:, STB + 512:STB + 520]
        Ca = ARBF[:, STB + 520:STB + 528]
        tmpc = ARBF[:, qf32 + T:qf32 + T + 256]
        tmpd = ARBF[:, qf32 + T + 256:qf32 + T + 512]

        def wrap_stage(W):
            t1 = T1[:, 0:W]
            t2 = T2[:, 0:W]
            emit("V", lambda e: e.tensor_scalar(t2, t1, pib, None,
                                                op0=OP.is_gt), drain=True)
            emit("V", lambda e: e.scalar_tensor_tensor(
                t1, t2, C1, t1, op0=OP.mult, op1=OP.subtract), drain=True)
            emit("V", lambda e: e.scalar_tensor_tensor(
                t1, t2, C2, t1, op0=OP.mult, op1=OP.add), drain=True)
            emit("V", lambda e: e.scalar_tensor_tensor(
                t1, t2, C3, t1, op0=OP.mult, op1=OP.add), drain=True)

        def rope_table(shift, out_slot, W, scale):
            t1 = T1[:, 0:W]
            t2 = T2[:, 0:W]
            sa0 = cnt["sA"]
            emit("V", lambda e: e.tensor_scalar(t1, T0[:, 0:W], invf, scale,
                                                op0=OP.mult, op1=OP.mult),
                 w=[("sA", sa0), ("sP", sP_done)], drain=True)
            if shift:
                emit("V", lambda e: e.tensor_scalar(t1, t1, halfpib, None,
                                                    op0=OP.add), drain=True)
            emit("V", lambda e: e.tensor_scalar(t2, t1, inv2pib, None,
                                                op0=OP.mult), drain=True)
            cw = min(64, W)
            for bb in range(W // cw):
                emit("V", lambda e, b_=bb, c_=cw: e.tensor_copy(
                    TI[:, 0:c_], T2[:, b_ * c_:b_ * c_ + c_]), drain=True)
                emit("V", lambda e, b_=bb, c_=cw: e.tensor_copy(
                    T2[:, b_ * c_:b_ * c_ + c_], TI[:, 0:c_]), drain=True)
            emit("V", lambda e: e.scalar_tensor_tensor(
                t1, t2, C1, t1, op0=OP.mult, op1=OP.subtract), drain=True)
            emit("V", lambda e: e.scalar_tensor_tensor(
                t1, t2, C2, t1, op0=OP.mult, op1=OP.add), drain=True)
            emit("V", lambda e: e.scalar_tensor_tensor(
                t1, t2, C3, t1, op0=OP.mult, op1=OP.add), drain=True)
            wrap_stage(W)
            wrap_stage(W)
            emit("V", lambda e: e.tensor_scalar(t2, t1, CLAMP, -CLAMP,
                                                op0=OP.min, op1=OP.max),
                 drain=True, inc="sV")
            vv = cnt["sV"]
            emit("A", lambda e, o=out_slot: e.activation(o, t2, AF.Sin,
                                                         scale=-1.0),
                 w=[("sV", vv)], inc="sA")
        rope_table(False, Sb, 256, 1.0)
        rope_table(True, Cb, 256, 1.0)
        rope_table(False, Sa, 8, 256.0)
        rope_table(True, Ca, 8, 256.0)
        tbl_a = cnt["sA"]
        for a in range(8):
            sl = slice(a * 256, a * 256 + 256)
            emit("V", lambda e, a_=a: e.tensor_scalar(
                tmpc, Cb, Sa[:, a_:a_ + 1], None, op0=OP.mult),
                w=[("sA", tbl_a)], drain=True)
            emit("V", lambda e, a_=a: e.tensor_scalar(
                tmpd, Sb, Ca[:, a_:a_ + 1], None, op0=OP.mult), drain=True)
            emit("V", lambda e, s=sl: e.tensor_tensor(
                sin16[:, s], tmpc, tmpd, op=OP.add), drain=True)
            emit("V", lambda e, a_=a: e.tensor_scalar(
                tmpc, Cb, Ca[:, a_:a_ + 1], None, op0=OP.mult), drain=True)
            emit("V", lambda e, a_=a: e.tensor_scalar(
                tmpd, Sb, Sa[:, a_:a_ + 1], None, op0=OP.mult), drain=True)
            emit("V", lambda e, s=sl: e.tensor_tensor(
                cos16[:, s], tmpc, tmpd, op=OP.subtract), drain=True,
                inc="sV")
        tbl_done = cnt["sV"]

        def layernorm(xn_of, prev_t, prev_v, prev_a):
            stats_hist = [prev_t, prev_t]
            stats_mm = prev_t
            for c in range(NCH):
                sl = (c % 2) * 2
                xb = actv(sl)
                xsq = actv(sl + 1)
                emit("A", lambda e, a=xb, c_=c: e.activation(
                    a, Xc(c_), AF.Copy),
                     w=[("qX", 16 * (c + 1)), ("sT", stats_hist[c % 2]),
                        ("sA", prev_a)], drain=True)
                emit("A", lambda e, a=xsq, c_=c: e.activation(
                    a, Xc(c_), AF.Square), inc="sA")
                aa_s = cnt["sA"]
                for fb in range(NFB):
                    emit("T", lambda e, f=fb, a=xb, c_=c: e.matmul(
                        PSB[0:1, f * 512:f * 512 + 512], onescol,
                        a[:, f * 512:f * 512 + 512],
                        start=(c_ == 0), stop=(c_ == NCH - 1)),
                        w=[("sA", aa_s), ("sP", sP_done), ("sV", prev_v)])
                    emit("T", lambda e, f=fb, a=xsq, c_=c: e.matmul(
                        PSA[0:1, f * 512:f * 512 + 512], onescol,
                        a[:, f * 512:f * 512 + 512],
                        start=(c_ == 0), stop=(c_ == NCH - 1)), inc="sT")
                stats_mm = cnt["sT"]
                stats_hist[c % 2] = stats_mm
            xn_done = []
            for fb in range(NFB):
                emit("V", lambda e, f=fb: e.tensor_scalar(
                    rowA, PSB[0:1, f * 512:f * 512 + 512], 1.0 / D, None,
                    op0=OP.mult), w=[("sT", stats_mm)], drain=True)
                emit("V", lambda e, f=fb: e.tensor_scalar(
                    rowB, PSA[0:1, f * 512:f * 512 + 512], 1.0 / D, 1e-5,
                    op0=OP.mult, op1=OP.add))
                emit("V", lambda e: e.tensor_tensor(rowM, rowA, rowA,
                                                    op=OP.mult), drain=True)
                emit("V", lambda e: e.tensor_tensor(rowB, rowB, rowM,
                                                    op=OP.subtract),
                    drain=True, inc="sV")
                vv = cnt["sV"]
                emit("A", lambda e: e.activation(rowE, rowB, AF.Ln),
                     w=[("sV", vv)], drain=True)
                emit("A", lambda e: e.activation(rowM16, rowE, AF.Exp,
                                                 scale=-0.5),
                     drain=True, inc="sA")
                aa = cnt["sA"]
                emit("V", lambda e: e.tensor_tensor(
                    rowNB16, rowA, rowM16, op=OP.mult),
                    w=[("sA", aa)], drain=True, inc="sV")
                vv2 = cnt["sV"]
                emit("T", lambda e, f=fb: e.matmul(
                    PSB[:, f * 512:f * 512 + 512], ones16r,
                    rowM16, start=True, stop=True),
                    w=[("sA", aa), ("sV", vv2), ("sP", sP_done)])
                emit("T", lambda e, f=fb: e.matmul(
                    PSA[:, f * 512:f * 512 + 512], ones16r,
                    rowNB16, start=True, stop=True), inc="sT")
                tt = cnt["sT"]
                for c in range(NCH):
                    emit("V", lambda e, f=fb, c_=c: e.tensor_tensor(
                        scr1, Xc(c_, f), PSB[:, f * 512:f * 512 + 512],
                        op=OP.mult), w=[("sT", tt)], drain=True)
                    emit("V", lambda e, f=fb, c_=c, xo=xn_of: e.tensor_tensor(
                        ARB[:, xo + c_ * T + f * 512:xo + c_ * T + f * 512 + 512],
                        scr1, PSA[:, f * 512:f * 512 + 512], op=OP.subtract),
                        drain=True, inc="sV")
                xn_done.append(cnt["sV"])
            return xn_done, cnt["sV"], stats_mm

        xn_fb, ln1_v, _ = layernorm(BO["act"], bc_t, cnt["sV"], tbl_a)

        flr = {"acc": ln1_v,
               "rot": ln1_v,
               "qpre_t": 0,
               "qpre_p": 0,
               "rsc0": 0,
               "rsc1": cnt["sP"]}
        qdone = {}
        mm_tail = {"t": 0}

        def tile_mm(wfn, col, fb, kc, nco_w):
            w = []
            if kc == 0:
                w = [("sV", flr["acc"]), ("sV", xn_fb[fb]), ("qW1", qW1_done)]
            emit("T", lambda e, k=kc, c_=col, f=fb, wf=wfn: e.matmul(
                FACC, wf(k, c_), actv(k, f),
                start=(k == 0), stop=(k == NCH - 1)), w=w,
                inc=("sT" if kc == NCH - 1 else None))
            if kc == NCH - 1:
                mm_tail["t"] = cnt["sT"]

        def tile_epi1(bias_ap, col):
            mm_t = mm_tail["t"]
            emit("V", lambda e, b=bias_ap, c_=col: e.tensor_scalar(
                qpre, FACC, b[:, c_:c_ + 1], None, op0=OP.add),
                w=[("sT", mm_t), ("sT", flr["qpre_t"]),
                   ("sP", flr["qpre_p"])], drain=True, inc="sV")
            flr["acc"] = cnt["sV"]

        def tile_epi2(fb, out_of, tag):
            emit("T", lambda e: e.matmul(FROT, p128t, qpre,
                                         start=True, stop=True),
                 w=[("sV", flr["acc"]), ("sV", flr["rot"]),
                    ("qC", qC_done)], inc="sT")
            flr["qpre_t"] = cnt["sT"]
            rot_t = cnt["sT"]
            emit("P", lambda e, f=fb: e.tensor_tensor(
                qpre, qpre, cos16[:, f * 512:f * 512 + 512], op=OP.mult),
                w=[("sV", flr["acc"]), ("sT", rot_t), ("sV", tbl_done)],
                drain=True, inc="sP")
            emit("V", lambda e, f=fb: e.tensor_tensor(
                rsc1, FROT, sin16[:, f * 512:f * 512 + 512], op=OP.mult),
                w=[("sT", rot_t), ("sP", flr["rsc1"]), ("sV", tbl_done)],
                drain=True, inc="sV")
            flr["rot"] = cnt["sV"]
            emit("P", lambda e, oo=out_of, f=fb: e.tensor_tensor(
                ARB[:, oo + f * 512:oo + f * 512 + 512],
                qpre, rsc1, op=OP.add),
                w=[("sV", flr["rot"])], drain=True, inc="sP")
            flr["qpre_p"] = cnt["sP"]
            flr["rsc1"] = cnt["sP"]
            qdone[(tag, fb)] = cnt["sP"]

        for ck in range(2):
            for fb in range(NFB):
                for kc in range(NCH):
                    tile_mm(wKc, ck, fb, kc, 2)
                tile_epi1(bk_t, ck)
                tile_epi2(fb, BO["k"] + ck * T, ("k", ck))
        for fb in range(NFB):
            for kc in range(NCH):
                tile_mm(wA, 0, fb, kc, NCH)
            tile_epi1(bq_t, 0)
            tile_epi2(fb, BO["q"], 0)
        k_done = max(qdone[(("k", c), f)] for c in range(2)
                     for f in range(NFB))

        vb_pp = [ln1_v, ln1_v]
        for j in range(T // 128):
            pp = j % 2
            pso = PSA[:, 1024 + pp * 512:1024 + pp * 512 + 256]
            for kc in range(NCH):
                w = [("sV", xn_fb[(j * 128) // 512]), ("sV", vb_pp[pp]),
                     ("qW1", qW1_done)] if kc == 0 else []
                emit("T", lambda e, k=kc, j_=j, ps=pso: e.matmul(
                    ps, actv(k)[:, j_ * 128:j_ * 128 + 128], wVc(k),
                    start=(k == 0), stop=False), w=w)
            emit("T", lambda e, ps=pso: e.matmul(
                ps, onesl, bvrow,
                start=False, stop=True), w=[("qC", qC_done)], inc="sT")
            tt = cnt["sT"]
            emit("V", lambda e, j_=j, ps=pso: e.tensor_copy(
                ARB[:, BO["v"] + j_ * 260:BO["v"] + j_ * 260 + 260]
                .rearrange("p (g d) -> p g d", g=4)[:, :, 0:64],
                ps.rearrange("p (g d) -> p g d", g=4)),
                w=[("sT", tt), ("sP", sP_done)], drain=True, inc="sV")
            vb_pp[pp] = cnt["sV"]
        v_done = cnt["sV"]
        kv_mm_done = cnt["sT"]

        emit("P", lambda e: e.dma_start(
            out=ARB[:, BO["wR1"]:BO["wR1"] + 8192].rearrange("p (kc o) -> p kc o", kc=8),
            in_=d_wo.rearrange("(kc p) o -> p kc o", p=128)),
            w=[("sT", cnt["sT"])], inc="qW2")
        assert cnt["qW2"] == qW2_done

        last_exp_psa = 0
        sc_guard = [tbl_a, tbl_a]
        slot_pv = [kv_mm_done] * 4
        pv_evac = [v_done, v_done]
        tail_bB = [0]
        evg = [0, 0]
        pending_tail = [None]
        pending_epi = [None]
        attn_mm_last = 0
        q_filler_last = kv_mm_done
        first_unit = [True]
        rowABs = ARF[32:33, LO["scr1"]:LO["scr1"] + 1024]

        def sc_head(g, hx, c_, q_, ck_):
            ba = 64 * hx
            for jj in range(2):
                j = 2 * g + jj
                w = []
                if jj == 0:
                    w = [("sA", sc_guard[hx])]
                    if g == 0:
                        w += [("sP", qdone[(c_, q_)]),
                              ("sP", k_done), ("sV", v_done)]
                        if first_unit[0]:
                            w += [("sV", ln1_v)]
                emit("T", lambda e, j_=j, jj_=jj, hx_=hx, ba_=ba:
                     e.matmul(
                    PSA[:, hx_ * 1024 + jj_ * 512:
                        hx_ * 1024 + jj_ * 512 + 512],
                    kv_(ck_, ba_, j_),
                    qv(c_, q_, ba_, 64), start=True, stop=True),
                    w=w,
                    inc=("sT" if jj == 1 else None))
            first_unit[0] = False
            return cnt["sT"]

        def pv_head(g, hx, exp_a, ck_):
            sl = 2 * hx + (g % 2)
            for jj in range(2):
                j = 2 * g + jj
                w = [("sA", exp_a)]
                if g == 0 and jj == 0:
                    w += [("sV", pv_evac[hx]), ("sP", sP_done)]
                emit("T", lambda e, j_=j, s=sl, jj_=jj, hx_=hx, k2=ck_:
                     e.matmul(
                    PSB[:, hx_ * 512:hx_ * 512 + 512],
                    vones(j_, 2 * k2 + hx_),
                    ARB[:, BO["esl"] + s * 1024 + jj_ * 512:
                        BO["esl"] + s * 1024 + jj_ * 512 + 512],
                    start=(j_ == 0), stop=(j_ == NJ - 1)), w=w,
                    inc=("sT" if jj == 1 else None))
            slot_pv[sl] = cnt["sT"]

        units = [(pr, qb) for pr in range(NCH) for qb in range(NQB)]
        hoisted = {}
        for ui, (pr, qb) in enumerate(units):
            ck = 0 if pr < 4 else 1
            fc = pr + 1
            par = ui % 2
            if pending_tail[0] is not None:
                pending_tail[0][0]()
            if (pr, qb) in hoisted:
                sc_t_next = hoisted.pop((pr, qb))
            else:
                sc_t_next = [sc_head(0, 0, pr, qb, ck),
                             sc_head(0, 1, pr, qb, ck)]
            if pending_epi[0] is not None:
                pending_epi[0][0]()
            for g in range(NG2):
                sc_t = sc_t_next
                exp_as = []
                for hx in range(2):
                    sl = 2 * hx + (g % 2)
                    emit("A", lambda e, s=sl, hx_=hx: e.activation(
                        ARB[:, BO["esl"] + s * 1024:
                            BO["esl"] + s * 1024 + 1024],
                        PSA[:, hx_ * 1024:hx_ * 1024 + 1024],
                        AF.Exp, scale=0.125),
                        w=[("sT", sc_t[hx]), ("sT", slot_pv[sl])],
                        inc="sA")
                    exp_as.append(cnt["sA"])
                    sc_guard[hx] = cnt["sA"]
                    last_exp_psa = cnt["sA"]
                if g == 1 and pending_tail[0] is not None:
                    pending_tail[0][1]()
                nxt = [0, 0]
                if g + 1 < NG2:
                    nxt[0] = sc_head(g + 1, 0, pr, qb, ck)
                    nxt[1] = sc_head(g + 1, 1, pr, qb, ck)
                elif ui + 1 < len(units):
                    pr2, qb2 = units[ui + 1]
                    ck2 = 0 if pr2 < 4 else 1
                    hoisted[(pr2, qb2)] = [
                        sc_head(0, 0, pr2, qb2, ck2),
                        sc_head(0, 1, pr2, qb2, ck2)]
                pv_head(g, 0, exp_as[0], ck)
                pv_tA = cnt["sT"]
                pv_head(g, 1, exp_as[1], ck)
                sc_t_next = nxt
                if fc < NCH:
                    tile_mm(wA, fc, qb, g, NCH)
                if g == 0 and pending_epi[0] is not None:
                    pending_epi[0][1]()
                    pending_epi[0] = None
                if g == 2 and pending_tail[0] is not None:
                    pending_tail[0][2]()
                if g == 3 and pending_tail[0] is not None:
                    pending_tail[0][3]()
                    pending_tail[0] = None
            pv_t = cnt["sT"]
            attn_mm_last = pv_t
            if fc < NCH:
                q_filler_last = mm_tail["t"]
                pending_epi[0] = (
                    lambda f_=fc: tile_epi1(bq_t, f_),
                    lambda f_=fc, q_=qb: tile_epi2(q_, BO["q"] + f_ * T, f_))

            tail_rc = [0]
            tail_cA = [0]
            tail_ev = [0]

            def tailEv(ptA=pv_tA, pt=pv_t, EV=EVb[par], par_=par):
                emit("V", lambda e: e.tensor_copy(
                    EV[:, 0:512], PSB[:, 0:512]),
                    w=[("sT", ptA), ("sV", evg[par_])],
                    drain=True, inc="sV")
                pv_evac[0] = cnt["sV"]
                emit("V", lambda e: e.tensor_copy(
                    EV[:, 512:1024], PSB[:, 512:1024]),
                    w=[("sT", pt)], drain=True, inc="sV")
                pv_evac[1] = cnt["sV"]
                tail_ev[0] = cnt["sV"]

            def tailD(EV=EVb[par]):
                emit("A", lambda e, r_=rowABs: e.activation(
                    r_, EV[64:65, :], AF.Ln), w=[("sV", tail_ev[0])])
                emit("A", lambda e, r_=rowABs: e.activation(
                    rowAB16, r_, AF.Exp, scale=-1.0),
                    w=[("sT", tail_bB[0])], drain=True, inc="sA")
                tail_rc[0] = cnt["sA"]

            def tailA(c_=pr, q_=qb, EV=EVb[par]):
                emit("T", lambda e: e.matmul(
                    FROT, ones16r, rowAB16[:, 0:512],
                    start=True, stop=True),
                    w=[("sA", tail_rc[0]), ("sV", flr["rot"])],
                    inc="sT")
                bA = cnt["sT"]
                emit("V", lambda e: e.tensor_copy(scr2, FROT),
                    w=[("sT", bA)], drain=True, inc="sV")
                tail_cA[0] = cnt["sV"]
                emit("V", lambda e: e.tensor_tensor(
                    qv(c_, q_, 0, 64), EV[0:64, 0:512],
                    scr2[0:64, :], op=OP.mult), drain=True, inc="sV")

            def tailB(c_=pr, q_=qb, EV=EVb[par], par_=par):
                emit("T", lambda e: e.matmul(
                    FROT, ones16r, rowAB16[:, 512:1024],
                    start=True, stop=True),
                    w=[("sV", tail_cA[0])], inc="sT")
                tail_bB[0] = cnt["sT"]
                emit("V", lambda e: e.tensor_copy(scr2, FROT),
                    w=[("sT", tail_bB[0])], drain=True, inc="sV")
                emit("V", lambda e: e.tensor_tensor(
                    qv(c_, q_, 64, 64), EV[0:64, 512:1024],
                    scr2[0:64, :], op=OP.mult), drain=True, inc="sV")
                evg[par_] = cnt["sV"]
                flr["rot"] = cnt["sV"]
            pending_tail[0] = (tailEv, tailD, tailA, tailB)
        if pending_epi[0] is not None:
            for fn in pending_epi[0]:
                fn()
            pending_epi[0] = None
        if pending_tail[0] is not None:
            for fn in pending_tail[0]:
                fn()
            pending_tail[0] = None
        ctx_done = cnt["sV"]

        emit("P", lambda e: e.dma_start(
            out=ARB[:, BO["wA"]:BO["wA"] + 8192].rearrange("p (kc o) -> p kc o", kc=8),
            in_=d_w1.rearrange("(kc p) o -> p kc o", p=128)),
            w=[("sT", q_filler_last)], inc="qW3")
        qW3_done = cnt["qW3"]
        emit("P", lambda e: e.dma_start(
            out=ARB[:, BO["w3"]:BO["w3"] + 8192].rearrange("p (kc o) -> p kc o", kc=8),
            in_=d_w3.rearrange("(kc p) o -> p kc o", p=128)),
            w=[("sT", attn_mm_last)], inc="qW5")
        qW5_done = cnt["qW5"]

        pp_x = [ctx_done, ctx_done]
        oproj_mm = 0
        for c in range(NCH):
            for fb in range(NFB):
                pp = (c * NFB + fb) % 2
                for kc in range(NCH):
                    w = [("qW2", qW2_done), ("sV", ctx_done),
                         ("sV", pp_x[pp]), ("sA", last_exp_psa)] \
                        if kc == 0 else []
                    emit("T", lambda e, k=kc, c_=c, f=fb, p=pp: e.matmul(
                        psA(p), wR1(k, c_), qv(k, f),
                        start=(k == 0), stop=(k == NCH - 1)), w=w,
                        inc=("sT" if kc == NCH - 1 else None))
                tt = cnt["sT"]
                oproj_mm = tt
                emit("V", lambda e, p=pp, c_=c, f=fb: e.scalar_tensor_tensor(
                    Xc(c_, f), psA(p), bo_t[:, c_:c_ + 1], Xc(c_, f),
                    op0=OP.add, op1=OP.add), w=[("sT", tt)], drain=True,
                    inc="sV")
                pp_x[pp] = cnt["sV"]
        x_attn = cnt["sV"]

        emit("P", lambda e: e.dma_start(
            out=ARB[:, BO["wR1"]:BO["wR1"] + 8192].rearrange("p (kc o) -> p kc o", kc=8),
            in_=d_w2.rearrange("(kc p) o -> p kc o", p=128)),
            w=[("sT", oproj_mm)], inc="qW4")
        qW4_done = cnt["qW4"]

        xn2_fb, ln2_v, ln2_mm = layernorm(BO["act"], oproj_mm, x_attn,
                                          last_exp_psa)

        def ffn_mm(wfn, rhs_of, out_of, wsem, wval, rhs_v, bank_guard):
            pp_g = [bank_guard, bank_guard]
            last = 0
            for c in range(NCH):
                for fb in range(NFB):
                    pp = (c * NFB + fb) % 2
                    for kc in range(NCH):
                        w = [(wsem, wval), ("sV", rhs_v), ("sV", pp_g[pp])] \
                            if kc == 0 else []
                        emit("T", lambda e, k=kc, c_=c, f=fb, p=pp, ro=rhs_of:
                             e.matmul(psA(p), wfn(k, c_),
                                      ARB[:, ro + k * T + f * 512:
                                          ro + k * T + f * 512 + 512],
                                      start=(k == 0), stop=(k == NCH - 1)),
                             w=w, inc=("sT" if kc == NCH - 1 else None))
                    tt = cnt["sT"]
                    last = tt
                    emit("V", lambda e, p=pp, c_=c, f=fb, oo=out_of:
                         e.tensor_copy(
                             ARB[:, oo + c_ * T + f * 512:
                                 oo + c_ * T + f * 512 + 512], psA(p)),
                         w=[("sT", tt)], drain=True, inc="sV")
                    pp_g[pp] = cnt["sV"]
            return last, cnt["sV"]

        f1_mm, h1_v = ffn_mm(wA, BO["act"], BO["q"], "qW3", qW3_done,
                             ln2_v, ln2_v)
        f2_mm, h2_v = ffn_mm(wR1, BO["q"], BO["act"], "qW4", qW4_done,
                             h1_v, h1_v)
        psa_guard_a = [0, 0, 0, 0]
        tsc_guard = [h2_v, h2_v]
        for c in range(NCH):
            for fb in range(NFB):
                idx = c * NFB + fb
                pp = idx % 4
                ts_ = scr1 if idx % 2 == 0 else scr2
                for kc in range(NCH):
                    w = [("qW5", qW5_done), ("sV", h2_v),
                         ("sA", psa_guard_a[pp])] \
                        if kc == 0 else []
                    emit("T", lambda e, k=kc, c_=c, f=fb, p=pp: e.matmul(
                        psA(p), wKV3(k, c_), actv(k, f),
                        start=(k == 0), stop=(k == NCH - 1)), w=w,
                        inc=("sT" if kc == NCH - 1 else None))
                tt = cnt["sT"]
                emit("A", lambda e, p=pp, c_=c, t_=ts_: e.activation(
                    t_, psA(p), AF.Tanh, bias=bfn_t[:, c_:c_ + 1]),
                    w=[("sT", tt), ("sV", tsc_guard[idx % 2])],
                    drain=True, inc="sA")
                aa = cnt["sA"]
                psa_guard_a[pp] = aa
                emit("V", lambda e, c_=c, f=fb, t_=ts_: e.tensor_tensor(
                    Xc(c_, f), Xc(c_, f), t_, op=OP.add),
                    w=[("sA", aa)], drain=True, inc="sV")
                tsc_guard[idx % 2] = cnt["sV"]
            emit("S", lambda e, c_=c: e.dma_start(
                out=d_y[c_ * 128:c_ * 128 + 128, :], in_=Xc(c_)),
                w=[("sV", cnt["sV"])], inc="qO")
        emit("S", lambda e: None, w=[("qO", cnt["qO"])])

        import contextlib
        with contextlib.ExitStack() as stack:
            sems = {s: stack.enter_context(nc.semaphore(s)) for s in SEMS}
            with nc.Block() as blk:
                def replay(key):
                    def body(eng):
                        for drain, ws, fn, inc in prog[key]:
                            if drain:
                                eng.drain(fusable=True)
                            for (s, v) in ws:
                                eng.wait_ge(sems[s], v)
                            ins = fn(eng)
                            if inc is not None:
                                ins.then_inc(sems[inc],
                                             16 if inc in DMA_SEMS else 1)
                    return body
                build._last = (prog, dict(cnt))
                blk.tensor(replay("T"))
                blk.vector(replay("V"))
                blk.scalar(replay("A"))
                blk.gpsimd(replay("P"))
                blk.sync(replay("S"))
    return nc


def _stage_consts(inputs):
    bf = ml_dtypes.bfloat16
    wq = np.asarray(inputs["L0_attn_Wq"], dtype=np.float32)
    wk = np.asarray(inputs["L0_attn_Wk"], dtype=np.float32)
    wv = np.asarray(inputs["L0_attn_Wv"], dtype=np.float32)
    wo = np.asarray(inputs["L0_attn_Wo"], dtype=np.float32)
    bq = np.asarray(inputs["L0_attn_bq"], dtype=np.float32)
    bk = np.asarray(inputs["L0_attn_bk"], dtype=np.float32)
    bv = np.asarray(inputs["L0_attn_bv"], dtype=np.float32)
    bo = np.asarray(inputs["L0_attn_bo"], dtype=np.float32)
    w1 = np.asarray(inputs["L0_ffn_weight"], dtype=np.float32)
    w2 = np.asarray(inputs["L0_ffn_aux_weight1"], dtype=np.float32)
    w3 = np.asarray(inputs["L0_ffn_aux_weight2"], dtype=np.float32)
    bfn = np.asarray(inputs["L0_ffn_bias"], dtype=np.float32)

    shared = {
        "wqT": np.ascontiguousarray(wq.T[:, SIGMA]),
        "wkT": np.ascontiguousarray(wk.T),
        "wvT": np.ascontiguousarray(wv.T),
        "woT": np.ascontiguousarray(wo.T[SIGMA, :]),
        "w1": np.ascontiguousarray(w1),
        "w2": np.ascontiguousarray(w2),
        "w3": np.ascontiguousarray(w3),
        "bq_t": np.ascontiguousarray(bq[SIGMA].reshape(8, 128).T),
        "bk_t": np.ascontiguousarray(bk.reshape(2, 128).T),
        "bo_t": np.ascontiguousarray(bo.reshape(8, 128).T),
        "bfn_t": np.ascontiguousarray(bfn.reshape(8, 128).T),
        "bv_row": np.ascontiguousarray(bv.reshape(1, 256)),
        "jmod": (np.arange(128, dtype=np.float32) % 32).reshape(128, 1),
    }
    p128 = np.zeros((128, 128), dtype=np.float32)
    for b in (0, 64):
        for i in range(32):
            p128[b + i, b + 2 * i + 1] = -1.0
            p128[b + 32 + i, b + 2 * i] = 1.0
    shared["p128t"] = np.ascontiguousarray(p128.T).astype(bf)
    return shared


def kernel(**inputs):
    T = 2048
    x = np.asarray(inputs["L0_attn_hidden_states"], dtype=np.float32)
    shared = _stage_consts(inputs)
    nc = build(T)
    in_maps = []
    for i in range(N_CORES):
        m = dict(shared)
        m["xT"] = np.ascontiguousarray(x[i].T)
        in_maps.append(m)
    res = run_bass_kernel_spmd(nc, in_maps, core_ids=list(range(N_CORES)))
    out = np.stack([np.asarray(r["yT"]).T for r in res.results], axis=0)
    return np.ascontiguousarray(out, dtype=np.float32)
